# version 01
# baseline (speedup 1.0000x reference)
import numpy as np
import ml_dtypes
from contextlib import ExitStack

import concourse.bass as bass
import concourse.mybir as mybir
from concourse.bass_utils import run_bass_kernel_spmd

F32 = mybir.dt.float32
BF16 = mybir.dt.bfloat16
FP8 = mybir.dt.float8e4
AX = mybir.AluOpType
AF = mybir.ActivationFunctionType
AXL = mybir.AxisListType

TC = 512
USE_PE_TRANSPOSE = True
FENCES = True
NCH = 8
NSEQ = 2
NTT = 4
D = 1024
DIN = 2048
NST = 16
E = 1024
SEQL = 4096
PAD = TC + 3
ZERO_HS = False
ZERO_CARRY = False
ZERO_TAIL = False


class Prog:

    ENG_SEM = {"tensor": "t", "vector": "v", "scalar": "s", "gpsimd": "g"}

    def __init__(self):
        self.seq = []
        self.res = {}
        self.by_eng = {"sync": [], "tensor": [], "vector": [], "scalar": [], "gpsimd": []}
        self.phase = {}

    def fence(self):
        for sm in list(self.phase):
            self.phase[sm] = self.phase[sm] + 1
        for sm in ("dma0", "dmt", "dmwX", "dmx", "gdma", "dm1", "dm2"):
            self.phase[sm] = self.phase.get(sm, 0) + 1
        lasts = [self.by_eng[e][-1] for e in self.by_eng if self.by_eng[e]]
        self._fence = (lasts, set(self.by_eng))

    def _apply_fence(self, eng, add_dep):
        f = getattr(self, "_fence", None)
        if f and eng in f[1]:
            f[1].discard(eng)
            for d in f[0]:
                add_dep(d)

    def close_phase(self, sem):
        if sem == "dmw":
            self.phase["dmwX"] = self.phase.get("dmwX", 0) + 1
            return
        self.phase[sem] = self.phase.get(sem, 0) + 1

    MERGE_DMA = False

    def op(self, eng, fn, reads=(), writes=(), dma=False, sem=None):
        if eng == "sync":
            dma = True
            sem = sem or "dma0"
            if sem == "dmw":
                sem = "dmw" if self.phase.get("dmwX", 0) % 2 == 0 else "dmw2"
                rec_ph = self.phase.get("dmwX", 0)
        elif eng == "gpsimd" and dma:
            sem = "gdma"
        else:
            sem = self.ENG_SEM[eng]
        rec = {"eng": eng, "fn": fn, "sem": sem, "dma": dma,
               "marked": False, "val": None, "wops": [],
               "ph": (locals().get("rec_ph") if sem in ("dmw", "dmw2")
                      else (self.phase.get(sem, 0) if dma else None))}
        seen = set()

        def add_dep(d):
            if id(d) not in seen and d is not rec:
                seen.add(id(d))
                rec["wops"].append(d)

        self._apply_fence(eng, add_dep)
        for r in reads:
            e = self.res.get(r)
            if e is None:
                e = self.res[r] = {"w": None, "r": []}
            if e["w"] is not None:
                add_dep(e["w"])
            e["r"].append(rec)
        for w in writes:
            e = self.res.get(w)
            if e is None:
                self.res[w] = {"w": rec, "r": []}
                continue
            if e["w"] is not None:
                add_dep(e["w"])
            for rr in e["r"]:
                add_dep(rr)
            self.res[w] = {"w": rec, "r": []}
        rec["gix"] = len(self.seq)
        self.seq.append(rec)
        self.by_eng[eng].append(rec)
        return rec

    def finalize(self):
        for rec in self.seq:
            if rec["dma"] and rec["fn"] is not None:
                rec["marked"] = True
            for dep in rec["wops"]:
                if dep["eng"] != rec["eng"] or dep["dma"]:
                    dep["marked"] = True
        counts = {}
        phase_end = {}
        for rec in self.seq:
            if rec["marked"]:
                amt = 16 if rec["dma"] else 1
                counts[rec["sem"]] = counts.get(rec["sem"], 0) + amt
                rec["val"] = counts[rec["sem"]]
                if rec["dma"]:
                    phase_end.setdefault(rec["sem"], {})[rec["ph"]] = rec["val"]
        self.sem_totals = counts
        self.phase_end = phase_end
        pmax = {}
        for rec in self.seq:
            if rec["marked"] and rec["dma"]:
                d = pmax.setdefault(rec["sem"], {})
                d[rec["ph"]] = max(d.get(rec["ph"], -1), rec["gix"])
        self.pwater = {}
        for sm, d in pmax.items():
            ends = self.phase_end[sm]
            self.pwater[sm] = sorted((gx, ends[ph]) for ph, gx in d.items() if ph in ends)

    def wait_val(self, dep, rec=None):
        if not dep["dma"]:
            return dep["val"]
        base = self.phase_end[dep["sem"]][dep["ph"]]
        if rec is None:
            return base
        v = base
        for gx, end in self.pwater.get(dep["sem"], ()):
            if gx < rec["gix"]:
                v = max(v, end)
            else:
                break
        assert v >= base
        return v

    def emit(self, eng_name, e, sems):
        waited = {}
        for rec in self.by_eng[eng_name]:
            for dep in rec["wops"]:
                if dep["eng"] == eng_name and not dep["dma"]:
                    continue
                if dep["val"] is None:
                    continue
                sm = dep["sem"]
                wv = self.wait_val(dep, rec)
                if waited.get(sm, -1) >= wv:
                    continue
                e.wait_ge(sems[sm], wv)
                waited[sm] = wv
            inst = rec["fn"](e)
            if rec["marked"]:
                amt = 16 if rec["dma"] else 1
                assert inst is not None, "marked op returned no instruction"
                inst.then_inc(sems[rec["sem"]], amt)


def build_nc(upto=4, debug=False):
    nc = bass.Bass("TRN2", target_bir_lowering=False)
    P = Prog()

    dx = nc.dram_tensor("x", [NSEQ, SEQL, D], F32, kind="ExternalInput").ap()
    dWin = nc.dram_tensor("a_W_in", [D, 2 * DIN], F32, kind="ExternalInput").ap()
    dWd = nc.dram_tensor("a_W_delta", [DIN, DIN], F32, kind="ExternalInput").ap()

    dWB = nc.dram_tensor("a_W_B", [DIN, NST], F32, kind="ExternalInput").ap()
    dWC = nc.dram_tensor("a_W_C", [DIN, NST], F32, kind="ExternalInput").ap()


    dWo = nc.dram_tensor("a_W_out", [DIN, D], F32, kind="ExternalInput").ap()
    dHin = nc.dram_tensor("b_w_in", [2 * E, D], F32, kind="ExternalInput").ap()

    dHg = nc.dram_tensor("b_w_gates", [2 * E, E], F32, kind="ExternalInput").ap()


    dHo = nc.dram_tensor("b_w_out", [D, E], F32, kind="ExternalInput").ap()
    dK = nc.dram_tensor("f_kernel", [D, D], BF16, kind="ExternalInput").ap()





    dconvd = nc.dram_tensor("c_convd", [128, 4096], F32, kind="ExternalInput").ap()
    dsmf = nc.dram_tensor("c_small_f", [128, 512], F32, kind="ExternalInput").ap()
    dsmb = nc.dram_tensor("c_small_b", [128, 4096], BF16, kind="ExternalInput").ap()
    dbrep = nc.dram_tensor("c_biasrep", [128, D], BF16, kind="ExternalInput").ap()


    dout = nc.dram_tensor("out", [NSEQ, SEQL, D], F32, kind="ExternalOutput").ap()
    ddbg = nc.dram_tensor("dbg", [8, D], F32, kind="ExternalOutput").ap() if debug else None
    ddbgw = nc.dram_tensor("dbgw", [4, D], BF16, kind="ExternalOutput").ap() if debug else None

    hWin = nc.dram_tensor("hWin", [D, 2 * DIN], BF16, kind="Internal").ap()
    hWd = nc.dram_tensor("hWd", [DIN, DIN], BF16, kind="Internal").ap()
    hWo = nc.dram_tensor("hWo", [DIN, D], BF16, kind="Internal").ap()
    hHin = nc.dram_tensor("hHin", [2 * E, D], BF16, kind="Internal").ap()
    hHg = nc.dram_tensor("hHg", [2 * E, E], BF16, kind="Internal").ap()
    hHo = nc.dram_tensor("hHo", [D, E], BF16, kind="Internal").ap()
    hHinT = nc.dram_tensor("hHinT", [D, 2 * E], BF16, kind="Internal").ap()
    hHgT = nc.dram_tensor("hHgT", [E, 2 * E], BF16, kind="Internal").ap()
    hHoT = nc.dram_tensor("hHoT", [E, D], BF16, kind="Internal").ap()
    hEcho = nc.dram_tensor("hEcho", [128, 128], BF16, kind="Internal").ap()

    es = ExitStack()
    sbt = lambda name, shape, dt: es.enter_context(nc.sbuf_tensor(name, shape, dt))

    csml_f = sbt("csml_f", [128, 512], F32)
    csml_b = sbt("csml_b", [128, 4096], BF16)
    convd = sbt("convd", [128, 4096], BF16)
    ffnq = sbt("ffnq", [128, 8 * D], BF16)
    wb_sb = sbt("wb_sb", [128, 16 * NST], BF16)
    wc_sb = sbt("wc_sb", [128, 16 * NST], BF16)
    a_neg = sbt("a_neg", [128, 16 * NST], F32)
    mfb = sbt("mfb", [128, 8], F32)
    rsin_col = sbt("rsin_col", [128, 1], F32)
    rsk_col = sbt("rsk_col", [128, 1], F32)
    scomb_col = sbt("scomb_col", [128, 1], F32)
    biasrep = sbt("biasrep", [128, D], BF16)
    hcar_m = sbt("hcar_m", [128, NSEQ * 256], F32)
    hcar_h = sbt("hcar_h", [128, NSEQ * 8], F32)
    xatail = sbt("xatail", [128, 24], BF16)
    junk = sbt("junk", [128, 1], F32)
    stats = sbt("stats", [128, 24], F32)
    bc_sb = sbt("bc_sb", [16, 2 * TC], BF16)
    cst = csml_f[:, 0:2]
    bdel = csml_f[:, 8:24]
    dpar = csml_f[:, 24:40]
    bconv = csml_f[:, 40:48]
    bgf = csml_f[:, 48:56]
    bgi = csml_f[:, 56:64]
    fbase = csml_f[:, 64:72]
    sin_rep = csml_f[:, 3:4]
    sk_rep = csml_f[:, 4:5]
    alog_src = csml_f[:, 256:512]
    ones_b = csml_b[0:1, 0:128]
    bias_sb = csml_b[0:1, 128:128 + D]
    ident = csml_b[:, 1152:1280]
    sel_sb = csml_b[0:16, 2048:4096]

    xres = sbt("xres", [128, NTT * D], F32)
    xhat_t = sbt("xhat_t", [128, NTT * D], BF16)
    xhat_f = sbt("xhat_f", [128, 8 * TC], BF16)

    arbf = sbt("arbf", [128, 25600], BF16)
    arf = sbt("arf", [128, 8192], F32)
    arf8 = sbt("arf8", [128, 8192], FP8)
    wa = sbt("wa", [128, 16384], BF16)

    u_bf = arbf[:, 0:8192]
    sres_bf = arbf[:, 8192:16384]
    y2_bf = arbf[:, 16384:24576]
    hC_b = arbf[:, 24576:25600]
    xqb = arbf[:, 0:4096]
    xq8 = arf8[:, 0:4096]
    kq8 = arf8[:, 0:8192]
    qtmp_b = arbf[:, 24576:25088]
    gate_bf = arbf[:, 0:4096]
    xa_pad = arbf[:, 4096:4096 + 8 * PAD]
    xc_bf = arbf[:, 8216:12312]
    sigi_bf = arbf[:, 12312:16408]
    hh_bf = arbf[:, 16408:16408 + 8 * (TC + 2)]
    yh_bf = arbf[:, 20520:24616]
    SP = TC + 2
    delta_f = arf[:, 0:2048]
    du_f = arf[:, 2048:4096]
    dA_f = arf[:, 4096:4096 + 2 * SP]
    h_f = arf[:, 5124:5124 + 2 * SP]
    s_a = arf[:, 6152:6664]
    s_r = arf[:, 6664:7176]
    uD_f = arf[:, 7176:7688]
    t2_f = arf[:, 0:512]
    lnscr = arf[:, 0:1024]
    alpha_f = arf[:, 0:8 * SP]
    beta_f = arf[:, 4112:4624]
    sigf_f = arf[:, 4624:5136]
    uh_f = arf[:, 5136:5136 + 2 * SP]
    ge_f = arf[:, 6164:6676]
    stg = wa[:, 0:2048]

    psum = [es.enter_context(nc.psum_tensor(f"ps{i}", [128, 512], F32))
            for i in range(8)]

    sems = {}
    for sname in ("dma0", "dmt", "dmw", "dmw2", "dmx", "gdma", "dm1", "dm2", "dmg", "t", "v", "s", "g"):
        sems[sname] = es.enter_context(nc.semaphore(sname))

    def mm(pbi, stat_ap, mov_ap, start, stop, extra_reads=(), psl=None):
        ptag = f"psum{pbi}"
        writes = (ptag,) if (start or stop) else ()
        reads = tuple(extra_reads)
        pb = psum[pbi][:, :] if psl is None else psl
        P.op("tensor",
             (lambda pb_, s_, m_, st_, sp_: lambda e: e.matmul(pb_, s_, m_,
                                                               start=st_, stop=sp_))(
                 pb, stat_ap, mov_ap, start, stop),
             reads=reads, writes=writes)

    P.op("sync", lambda e: e.dma_start(csml_f[:, :], dsmf), writes=("csml_f",))
    P.op("sync", lambda e: e.dma_start(csml_b[:, :], dsmb), writes=("csml_b",))
    P.op("sync", lambda e: e.dma_start(ffnq[:, :].rearrange("p (k c) -> p k c", c=D),
                                       dK.rearrange("(k p) c -> p k c", p=128)),
         writes=("ffnq",))
    P.op("sync", lambda e: e.dma_start(biasrep[:, :], dbrep), writes=("biasrep",))
    P.close_phase("dma0")

    wld_f = wa[:, 0:8192].bitcast(F32)
    wld_b = wa[:, 8192:12288]
    P.op("sync", lambda e: e.dma_start(wld_f[:, 0:4096], dconvd),
         writes=("wld",), sem="dm1")
    P.close_phase("dm1")
    P.op("scalar", lambda e: e.copy(convd[:, :], wld_f[:, 0:4096]),
         reads=("wld",), writes=("convd",))
    P.op("sync", lambda e: e.dma_start(
        wld_f[:, 0:256].rearrange("p (k n) -> p k n", n=NST),
        dWB.rearrange("(k p) n -> p k n", p=128),
        allow_slow_non_contiguous=True), reads=("convd",), writes=("wldB",), sem="dm1")
    P.op("sync", lambda e: e.dma_start(
        wld_f[:, 256:512].rearrange("p (k n) -> p k n", n=NST),
        dWC.rearrange("(k p) n -> p k n", p=128),
        allow_slow_non_contiguous=True), reads=("convd",), writes=("wldC",), sem="dm1")
    P.close_phase("dm1")
    P.op("scalar", lambda e: e.copy(wb_sb[:, :], wld_f[:, 0:256]),
         reads=("wldB",), writes=("wb_sb",))
    P.op("scalar", lambda e: e.copy(wc_sb[:, :], wld_f[:, 256:512]),
         reads=("wldC",), writes=("wc_sb",))

    def cast_matrix(hsrc_f32, hdst_bf, R, C):
        dn = f"hbm_{hdst_bf.tensor.name}"
        for rr in range(R // 128):
            cw = 0
            while cw < C:
                w = min(4096, C - cw)
                P.op("sync", (lambda rr_, cw_, w_: lambda e: e.dma_start(
                    wld_f[:, 0:w_],
                    hsrc_f32[rr_ * 128:(rr_ + 1) * 128, cw_:cw_ + w_]))(rr, cw, w),
                    writes=("wld", "wldB", "wldC"), sem="dm1")
                P.close_phase("dm1")
                P.op("scalar", (lambda w_: lambda e: e.copy(
                    wld_b[:, 0:w_], wld_f[:, 0:w_]))(w),
                    reads=("wld",), writes=("wld_b",))
                P.op("sync", (lambda rr_, cw_, w_: lambda e: e.dma_start(
                    hdst_bf[rr_ * 128:(rr_ + 1) * 128, cw_:cw_ + w_],
                    wld_b[:, 0:w_]))(rr, cw, w),
                    reads=("wld_b",), writes=(f"{dn}_{rr}",), sem="dm2")
                P.close_phase("dm2")
                cw += w

    cast_matrix(dWin, hWin, D, 2 * DIN)
    cast_matrix(dWd, hWd, DIN, DIN)
    cast_matrix(dWo, hWo, DIN, D)

    def cast_transpose_matrix(hsrc_f32, hdst_bf, R, C):
        dn = f"hbm_{hdst_bf.tensor.name}"
        acc = arbf[:, 0:(C // 128) * R]
        for rr in range(R // 128):
            P.op("sync", (lambda rr_: lambda e: e.dma_start(
                wld_f[:, 0:C], hsrc_f32[rr_ * 128:(rr_ + 1) * 128, :]))(rr),
                writes=("wld", "wldB", "wldC"), sem="dm1")
            P.close_phase("dm1")
            P.op("scalar", lambda e: e.copy(wld_b[:, 0:C], wld_f[:, 0:C]),
                 reads=("wld",), writes=("wld_b",))
            for cc in range(C // 128):
                pbi = cc % 2
                P.op("tensor", (lambda cc_, pb_: lambda e: e.transpose(
                    psum[pb_][:, :].bitcast(BF16)[:, 0:128],
                    wld_b[:, cc_ * 128:(cc_ + 1) * 128], ident))(cc, pbi),
                    reads=("wld_b", "csml_b"), writes=(f"psum{pbi}",))
                P.op("scalar", (lambda rr_, cc_, pb_: lambda e: e.copy(
                    acc[:, cc_ * R + rr_ * 128: cc_ * R + (rr_ + 1) * 128],
                    psum[pb_][:, :].bitcast(BF16)[:, 0:128]))(rr, cc, pbi),
                    reads=(f"psum{pbi}",), writes=(f"tacc{cc}",))
        for cc in range(C // 128):
            P.op("sync", (lambda cc_: lambda e: e.dma_start(
                hdst_bf[cc_ * 128:(cc_ + 1) * 128, :], acc[:, cc_ * R:(cc_ + 1) * R]))(cc),
                reads=(f"tacc{cc}",), writes=(f"{dn}{cc}",), sem="dm2")
        P.close_phase("dm2")

    cast_transpose_matrix(dHin, hHinT, 2 * E, D)
    cast_transpose_matrix(dHg, hHgT, 2 * E, E)
    cast_transpose_matrix(dHo, hHoT, D, E)

    if debug:
        for row, hsrc, rtag in ((0, hWin, "hbm_hWin_0"), (1, hHinT, "hbm_hHinT0"),
                                (2, hWd, "hbm_hWd_0"), (3, hHgT, "hbm_hHgT0")):
            P.op("sync", (lambda row_, h_: lambda e: e.dma_start(
                ddbgw[row_:row_ + 1, :], h_[0:1, 0:D]))(row, hsrc),
                reads=(rtag,), writes=(f"dbgw{row}",), sem="dmg")
        P.close_phase("dma0")

    for ei, (h_, rt) in enumerate(((hWin, "hbm_hWin_7"), (hWd, "hbm_hWd_15"),
                                   (hWo, "hbm_hWo_15"), (hHinT, "hbm_hHinT7"),
                                   (hHgT, "hbm_hHgT7"), (hHoT, "hbm_hHoT7"))):
        P.op("sync", (lambda hh, ei_: lambda e: e.dma_start(
            wld_b[:, ei_ * 64:(ei_ + 1) * 64], hh[hh.shape[0] - 128:, 0:64]))(h_, ei),
            reads=(rt,) + (("wld", "wldB", "wldC") if ei == 0 else ()),
            writes=(f"echo_scr{ei}",))
    P.close_phase("dma0")

    for fn_ in (AF.Exp, AF.Ln, AF.Silu, AF.Sigmoid, AF.Gelu, AF.Square, AF.Abs, AF.Relu):
        P.op("scalar", (lambda f2: lambda e: e.activation(
            mfb[:, 0:8], csml_f[:, 2:3].broadcast_to([128, 8]) if False else csml_f[:, 8:16],
            f2))(fn_), reads=("csml_f",), writes=("mfb",))

    P.op("scalar", lambda e: e.activation(a_neg[:, :], alog_src, AF.Exp),
         reads=("csml_f",), writes=("a_neg",))
    P.op("scalar", lambda e: e.activation(a_neg[:, :], a_neg[:, :], AF.Copy, scale=-1.0),
         reads=("a_neg",), writes=("a_neg",))
    P.op("scalar", lambda e: e.activation(mfb[:, :], fbase, AF.Abs),
         reads=("csml_f",), writes=("mfb",))
    P.op("scalar", lambda e: e.activation(mfb[:, :], mfb[:, :], AF.Exp, scale=-1.0),
         reads=("mfb",), writes=("mfb",))
    P.op("scalar", lambda e: e.activation(mfb[:, :], mfb[:, :], AF.Ln, bias=1.0),
         reads=("mfb",), writes=("mfb",))
    P.op("vector", lambda e: e.scalar_tensor_tensor(mfb[:, :], fbase, 0.0, mfb[:, :],
                                                    AX.max, AX.add),
         reads=("csml_f", "mfb"), writes=("mfb",))
    P.op("scalar", lambda e: e.activation(mfb[:, :], mfb[:, :], AF.Copy, scale=-8.0),
         reads=("mfb",), writes=("mfb",))
    P.op("vector", lambda e: e.reciprocal(rsin_col[:, 0:1], sin_rep),
         reads=("csml_f",), writes=("rsin_col",))
    P.op("vector", lambda e: e.reciprocal(rsk_col[:, 0:1], sk_rep),
         reads=("csml_f",), writes=("rsk_col",))
    P.op("vector", lambda e: e.tensor_mul(scomb_col[:, 0:1], sin_rep, sk_rep),
         reads=("csml_f",), writes=("scomb_col",))

    if debug:
        P.op("sync", lambda e: e.dma_start(ddbg[0:1, 0:1], scomb_col[0:1, 0:1]),
             reads=("scomb_col",), writes=("dbg_done1",), sem="dmg")
        P.op("sync", lambda e: e.dma_start(ddbg[1:2, :].bitcast(BF16)[:, 0:D], bias_sb),
             reads=("csml_b",), writes=("dbg_done2",), sem="dmg")
        P.op("sync", lambda e: e.dma_start(ddbg[3:4, 0:1], rsin_col[0:1, 0:1]),
             reads=("rsin_col",), writes=("dbg_done4",), sem="dmg")
        P.op("sync", lambda e: e.dma_start(ddbg[3:4, 1:2], scomb_col[0:1, 0:1]),
             reads=("scomb_col",), writes=("dbg_done5",), sem="dmg")
        P.close_phase("dma0")

    P.op("vector", lambda e: e.tensor_scalar(ffnq[:, :], ffnq[:, :], rsk_col[:, 0:1],
                                             240.0, AX.mult, AX.min),
         reads=("ffnq", "rsk_col"), writes=("ffnq",))
    P.op("vector", lambda e: e.tensor_scalar(kq8[:, :], ffnq[:, :], -240.0, None, AX.max),
         reads=("ffnq",), writes=("kq8",))
    P.op("scalar", lambda e: e.copy(ffnq[:, :], kq8[:, :]),
         reads=("kq8",), writes=("ffnq",))
    if debug:
        P.op("sync", lambda e: e.dma_start(ddbg[3:4, :].bitcast(BF16)[:, 512:1536],
                                           ffnq[0:1, 0:1024]),
             reads=("ffnq",), writes=("dbg_done6b",), sem="dmg")

    P.fence()

    def layernorm():
        for tt in range(NTT):
            P.op("vector", (lambda tt_: lambda e: e.reduce_sum(
                stats[:, tt_:tt_ + 1], xres[:, tt_ * D:(tt_ + 1) * D], AXL.X))(tt),
                reads=(f"xres{tt}",), writes=(f"st_sx{tt}",))
            P.op("scalar", (lambda tt_: lambda e: e.activation(
                lnscr[:, :], xres[:, tt_ * D:(tt_ + 1) * D], AF.Square,
                accum_out=stats[:, 4 + tt_:5 + tt_]))(tt),
                reads=(f"xres{tt}",), writes=("lnscr", f"st_sq{tt}"))
        rd = tuple(f"st_sx{t}" for t in range(NTT)) + tuple(f"st_sq{t}" for t in range(NTT))
        P.op("vector", lambda e: e.tensor_scalar(stats[:, 8:12], stats[:, 0:4],
                                                 1.0 / D, None, AX.mult),
             reads=rd, writes=("st_mean",))
        P.op("vector", lambda e: e.tensor_scalar(stats[:, 12:16], stats[:, 4:8],
                                                 1.0 / D, None, AX.mult),
             reads=rd, writes=("st_var",))
        P.op("scalar", lambda e: e.activation(stats[:, 16:20], stats[:, 8:12], AF.Square),
             reads=("st_mean",), writes=("st_t",))
        P.op("vector", lambda e: e.tensor_sub(stats[:, 12:16], stats[:, 12:16],
                                              stats[:, 16:20]),
             reads=("st_var", "st_t"), writes=("st_var",))
        P.op("scalar", lambda e: e.activation(stats[:, 12:16], stats[:, 12:16],
                                              AF.Ln, bias=cst[:, 0:1]),
             reads=("st_var", "csml_f"), writes=("st_var",))
        P.op("scalar", lambda e: e.activation(stats[:, 12:16], stats[:, 12:16],
                                              AF.Exp, scale=-0.5),
             reads=("st_var",), writes=("st_var",))
        P.op("scalar", lambda e: e.activation(stats[:, 20:24], stats[:, 8:12],
                                              AF.Copy, scale=-1.0),
             reads=("st_mean",), writes=("st_negm",))
        for tt in range(NTT):
            P.op("vector", (lambda tt_: lambda e: e.tensor_scalar(
                xhat_t[:, tt_ * D:(tt_ + 1) * D], xres[:, tt_ * D:(tt_ + 1) * D],
                stats[:, 20 + tt_:21 + tt_], stats[:, 12 + tt_:13 + tt_],
                AX.add, AX.mult))(tt),
                reads=(f"xres{tt}", "st_var", "st_negm"), writes=(f"xhat_t{tt}",))
        if USE_PE_TRANSPOSE:
            for tt in range(NTT):
                for dd in range(8):
                    pbi = dd % 2
                    P.op("tensor", (lambda tt_, dd_, pb_: lambda e: e.transpose(
                        psum[pb_][:, :].bitcast(BF16)[:, 0:128],
                        xhat_t[:, tt_ * D + dd_ * 128: tt_ * D + (dd_ + 1) * 128],
                        ident))(tt, dd, pbi),
                        reads=(f"xhat_t{tt}", "csml_b"), writes=(f"psum{pbi}",))
                    P.op("scalar", (lambda tt_, dd_, pb_: lambda e: e.copy(
                        xhat_f[:, dd_ * TC + tt_ * 128: dd_ * TC + (tt_ + 1) * 128],
                        psum[pb_][:, :].bitcast(BF16)[:, 0:128]))(tt, dd, pbi),
                        reads=(f"psum{pbi}",), writes=(f"xhat_f{tt}",))
        else:
            for tt in range(NTT):
                P.op("sync", (lambda tt_: lambda e: e.dma_start(
                    xhat_f[:, :].rearrange("p (dd t) -> p dd t", t=TC)[:, :, tt_ * 128:(tt_ + 1) * 128],
                    xhat_t[:, tt_ * D:(tt_ + 1) * D], transpose=True))(tt),
                    reads=(f"xhat_t{tt}",), writes=(f"xhat_f{tt}",), sem="dmt")
            P.close_phase("dmt")

    XHATF = tuple(f"xhat_f{t}" for t in range(NTT))

    dbg_state = {"n": 0}

    def ffn_sublayer():
        layernorm()
        if debug and dbg_state["n"] == 0:
            dbg_state["n"] = 1
            P.op("sync", lambda e: e.dma_start(ddbg[4:5, :].bitcast(BF16)[:, 0:D],
                                               xhat_t[0:1, 0:D]),
                 reads=("xhat_t0",), writes=("dbg_done6",), sem="dmg")
            P.op("sync", lambda e: e.dma_start(ddbg[5:6, :].bitcast(BF16)[:, 0:TC],
                                               xhat_f[0:1, 0:TC]),
                 reads=("xhat_f0", "xhat_f1", "xhat_f2", "xhat_f3"), writes=("dbg_done7",), sem="dmg")
        for dd in range(8):
            P.op("vector", (lambda dd_: lambda e: e.tensor_scalar(
                qtmp_b[:, :], xhat_f[:, dd_ * TC:(dd_ + 1) * TC], rsin_col[:, 0:1],
                240.0, AX.mult, AX.min))(dd),
                reads=XHATF + ("rsin_col",), writes=("qtmp",))
            P.op("vector", (lambda dd_: lambda e: e.tensor_scalar(
                xq8[:, dd_ * TC:(dd_ + 1) * TC], qtmp_b[:, :], -240.0, None, AX.max))(dd),
                reads=("qtmp",), writes=(f"xq8_{dd}",))
            P.op("scalar", (lambda dd_: lambda e: e.copy(
                xqb[:, dd_ * TC:(dd_ + 1) * TC], xq8[:, dd_ * TC:(dd_ + 1) * TC]))(dd),
                reads=(f"xq8_{dd}",), writes=(f"xqb_{dd}",))
        if debug and dbg_state["n"] == 1:
            dbg_state["n"] = 2
            P.op("sync", lambda e: e.dma_start(ddbg[6:7, :].bitcast(BF16)[:, 0:TC],
                                               xqb[0:1, 0:TC]),
                 reads=("xqb_0",), writes=("dbg_done8",), sem="dmg")
        for tt in range(NTT):
            for ch in range(2):
                pbi = 2 + ch
                for dd in range(8):
                    mm(pbi, xqb[:, dd * TC + tt * 128: dd * TC + (tt + 1) * 128],
                       ffnq[:, dd * D + ch * 512: dd * D + ch * 512 + 512],
                       dd == 0, dd == 7, extra_reads=(f"xqb_{dd}", "ffnq"))
                P.op("vector", (lambda pb_, tt_, ch_: lambda e: e.scalar_tensor_tensor(
                    xres[:, tt_ * D + ch_ * 512: tt_ * D + ch_ * 512 + 512],
                    psum[pb_][:, :], scomb_col[:, 0:1],
                    xres[:, tt_ * D + ch_ * 512: tt_ * D + ch_ * 512 + 512],
                    AX.mult, AX.add))(pbi, tt, ch),
                    reads=(f"psum{pbi}", f"xres{tt}", "scomb_col"),
                    writes=(f"xres{tt}",))
                P.op("vector", (lambda tt_, ch_: lambda e: e.tensor_add(
                    xres[:, tt_ * D + ch_ * 512: tt_ * D + ch_ * 512 + 512],
                    xres[:, tt_ * D + ch_ * 512: tt_ * D + ch_ * 512 + 512],
                    biasrep[:, ch_ * 512:(ch_ + 1) * 512]))(tt, ch),
                    reads=(f"xres{tt}", "biasrep"), writes=(f"xres{tt}",))

    def chunk(s, c):
        t0 = c * TC
        for tt in range(NTT):
            P.op("sync", (lambda tt_: lambda e: e.dma_start(
                xres[:, tt_ * D:(tt_ + 1) * D],
                dx[s, t0 + tt_ * 128: t0 + (tt_ + 1) * 128, :]))(tt),
                reads=(f"xres{tt}",), writes=(f"xres{tt}",), sem="dmx")
        P.close_phase("dmx")

        layernorm()
        for pb in range(2):
            P.op("vector", (lambda pb_: lambda e: e.memset(
                dA_f[:, pb_ * SP:pb_ * SP + 2], 0.0))(pb),
                writes=(f"dA{pb}",))
            P.op("vector", (lambda pb_: lambda e: e.memset(
                h_f[:, pb_ * SP:pb_ * SP + 1], 0.0))(pb),
                writes=(f"h{pb}",))
        for half in range(2):
            for dd in range(8):
                P.op("sync", (lambda dd_, h_: lambda e: e.dma_start(
                    wa[:, dd_ * 2048:(dd_ + 1) * 2048],
                    hWin[dd_ * 128:(dd_ + 1) * 128, h_ * 2048:(h_ + 1) * 2048]))(dd, half),
                    reads=(f"hbm_hWin_{dd}",), writes=(f"wa{dd}",), sem="dmw")
            P.close_phase("dmw")
            dst = u_bf if half == 0 else sres_bf
            dtag = "u_bf" if half == 0 else "sres_bf"
            for ffl in range(16):
                pbi = ffl % 2
                for dd in range(8):
                    mm(pbi, wa[:, dd * 2048 + ffl * 128: dd * 2048 + (ffl + 1) * 128],
                       xhat_f[:, dd * TC:(dd + 1) * TC], dd == 0, dd == 7,
                       extra_reads=(f"wa{dd}",) + XHATF)
                P.op("scalar", (lambda pb_, dst_, ffl_: lambda e: e.activation(
                    dst_[:, ffl_ * TC:(ffl_ + 1) * TC], psum[pb_][:, :], AF.Silu))(pbi, dst, ffl),
                    reads=(f"psum{pbi}",), writes=(f"{dtag}_{ffl}",))

        for k in range(16):
            mm(7, wb_sb[:, k * 16:(k + 1) * 16], u_bf[:, k * TC:(k + 1) * TC],
               k == 0, k == 15, extra_reads=("wb_sb", f"u_bf_{k}"),
               psl=psum[7][0:16, :])
        P.op("vector", lambda e: e.tensor_copy(bc_sb[0:16, 0:TC], psum[7][0:16, :]),
             reads=("psum7",), writes=("B_sb",))
        for k in range(16):
            mm(7, wc_sb[:, k * 16:(k + 1) * 16], u_bf[:, k * TC:(k + 1) * TC],
               k == 0, k == 15, extra_reads=("wc_sb", f"u_bf_{k}"),
               psl=psum[7][0:16, :])
        P.op("vector", lambda e: e.tensor_copy(bc_sb[0:16, TC:2 * TC], psum[7][0:16, :]),
             reads=("psum7",), writes=("C_sb",))

        for half in range(2):
            for kk in range(16):
                P.op("sync", (lambda kk_, h_: lambda e: e.dma_start(
                    wa[:, kk_ * 1024:(kk_ + 1) * 1024],
                    hWd[kk_ * 128:(kk_ + 1) * 128, h_ * 1024:(h_ + 1) * 1024]))(kk, half),
                    reads=(f"hbm_hWd_{kk}",), writes=(f"wa{kk}",), sem="dmw")
            P.close_phase("dmw")
            for blk in range(2):
                dts = [half * 8 + blk * 4 + i for i in range(4)]
                for i, dt in enumerate(dts):
                    dtl = dt - half * 8
                    for k in range(16):
                        mm(4, wa[:, k * 1024 + dtl * 128: k * 1024 + (dtl + 1) * 128],
                           u_bf[:, k * TC:(k + 1) * TC], k == 0, k == 15,
                           extra_reads=(f"wa{k}", f"u_bf_{k}"))
                    P.op("scalar", (lambda dt_: lambda e: e.activation(
                        s_a[:, :], psum[4][:, :], AF.Abs, bias=bdel[:, dt_:dt_ + 1]))(dt),
                        reads=("psum4", "csml_f"), writes=("s_a",))
                    P.op("scalar", lambda e: e.activation(s_a[:, :], s_a[:, :],
                                                          AF.Exp, scale=-1.0),
                         reads=("s_a",), writes=("s_a",))
                    P.op("scalar", lambda e: e.activation(s_a[:, :], s_a[:, :],
                                                          AF.Ln, bias=1.0),
                         reads=("s_a",), writes=("s_a",))
                    P.op("scalar", (lambda dt_: lambda e: e.activation(
                        s_r[:, :], psum[4][:, :], AF.Relu, bias=bdel[:, dt_:dt_ + 1]))(dt),
                        reads=("psum4", "csml_f"), writes=("s_r",))
                    P.op("vector", (lambda i_: lambda e: e.tensor_add(
                        delta_f[:, i_ * TC:(i_ + 1) * TC], s_a[:, :], s_r[:, :]))(i),
                        reads=("s_a", "s_r"), writes=(f"delta{i}",))
                    P.op("vector", (lambda i_, dt_: lambda e: e.tensor_mul(
                        du_f[:, i_ * TC:(i_ + 1) * TC],
                        delta_f[:, i_ * TC:(i_ + 1) * TC],
                        u_bf[:, dt_ * TC:(dt_ + 1) * TC]))(i, dt),
                        reads=(f"delta{i}", f"u_bf_{dt}"), writes=(f"du{i}",))
                for n in range(16):
                    bB = 4 if n % 2 == 0 else 5
                    bC = 6 if n % 2 == 0 else 7
                    mm(bB, sel_sb[:, n * 128:(n + 1) * 128], bc_sb[0:16, 0:TC],
                       True, True, extra_reads=("csml_b", "B_sb"))
                    mm(bC, sel_sb[:, n * 128:(n + 1) * 128], bc_sb[0:16, TC:2 * TC],
                       True, True, extra_reads=("csml_b", "C_sb"))
                    for i, dt in enumerate(dts):
                        pbuf = (n * 4 + i) % 2
                        car = hcar_m[:, s * 256 + dt * 16 + n: s * 256 + dt * 16 + n + 1]
                        P.op("vector", (lambda pb_, car_: lambda e: e.tensor_copy(
                            h_f[:, pb_ * SP + 1:pb_ * SP + 2], car_))(pbuf, car),
                            reads=(f"car_{dt}_{n}",), writes=(f"h{pbuf}",))
                        P.op("scalar", (lambda i_, dt_, n_, pb_: lambda e: e.activation(
                            dA_f[:, pb_ * SP + 2:(pb_ + 1) * SP],
                            delta_f[:, i_ * TC:(i_ + 1) * TC], AF.Exp,
                            scale=a_neg[:, dt_ * 16 + n_:dt_ * 16 + n_ + 1]))(i, dt, n, pbuf),
                            reads=(f"delta{i}", "a_neg"), writes=(f"dA{pbuf}",))
                        P.op("vector", (lambda i_, pb_, bB_: lambda e: e.tensor_mul(
                            h_f[:, pb_ * SP + 2:(pb_ + 1) * SP],
                            du_f[:, i_ * TC:(i_ + 1) * TC],
                            psum[bB_][:, :]))(i, pbuf, bB),
                            reads=(f"du{i}", f"psum{bB}"), writes=(f"h{pbuf}",))
                        P.op("vector", (lambda pb_: lambda e: e.tensor_tensor_scan(
                            h_f[:, pb_ * SP:(pb_ + 1) * SP],
                            dA_f[:, pb_ * SP:(pb_ + 1) * SP],
                            h_f[:, pb_ * SP:(pb_ + 1) * SP],
                            junk[:, 0:1], AX.mult, AX.add))(pbuf),
                            reads=(f"dA{pbuf}", f"h{pbuf}"),
                            writes=(f"h{pbuf}",))
                        P.op("vector", (lambda pb_, bC_: lambda e: e.tensor_mul(
                            hC_b[:, pb_ * TC:(pb_ + 1) * TC],
                            h_f[:, pb_ * SP + 2:(pb_ + 1) * SP],
                            psum[bC_][:, :]))(pbuf, bC),
                            reads=(f"h{pbuf}", f"psum{bC}"), writes=(f"hC{pbuf}",))
                        P.op("vector", (lambda pb_, car_: lambda e: e.tensor_copy(
                            car_, h_f[:, (pb_ + 1) * SP - 1:(pb_ + 1) * SP]))(pbuf, car),
                            reads=(f"h{pbuf}",), writes=(f"car_{dt}_{n}",))
                        mm(i, ident[:, :], hC_b[:, pbuf * TC:(pbuf + 1) * TC],
                           n == 0, n == 15, extra_reads=("csml_b", f"hC{pbuf}"))
                for i, dt in enumerate(dts):
                    P.op("scalar", (lambda dt_: lambda e: e.activation(
                        uD_f[:, :], u_bf[:, dt_ * TC:(dt_ + 1) * TC], AF.Copy,
                        scale=dpar[:, dt_:dt_ + 1]))(dt),
                        reads=(f"u_bf_{dt}", "csml_f"), writes=("uD_f",))
                    P.op("vector", (lambda i_: lambda e: e.tensor_add(
                        t2_f[:, :], psum[i_][:, :], uD_f[:, :]))(i),
                        reads=(f"psum{i}", "uD_f"), writes=("t2_f",))
                    P.op("vector", (lambda dt_: lambda e: e.tensor_mul(
                        y2_bf[:, dt_ * TC:(dt_ + 1) * TC], t2_f[:, :],
                        sres_bf[:, dt_ * TC:(dt_ + 1) * TC]))(dt),
                        reads=("t2_f", f"sres_bf_{dt}"), writes=(f"y2_{dt}",))

        for kk in range(16):
            P.op("sync", (lambda kk_: lambda e: e.dma_start(
                wa[:, kk_ * 1024:(kk_ + 1) * 1024],
                hWo[kk_ * 128:(kk_ + 1) * 128, :]))(kk),
                reads=(f"hbm_hWo_{kk}",), writes=(f"wa{kk}",), sem="dmw")
        P.close_phase("dmw")
        for tt in range(NTT):
            for ch in range(2):
                pbi = 2 + ch
                for k in range(16):
                    mm(pbi, y2_bf[:, k * TC + tt * 128: k * TC + (tt + 1) * 128],
                       wa[:, k * 1024 + ch * 512: k * 1024 + ch * 512 + 512],
                       k == 0, k == 15,
                       extra_reads=(f"y2_{k}", f"wa{k}"))
                P.op("vector", (lambda pb_, tt_, ch_: lambda e: e.scalar_tensor_tensor(
                    xres[:, tt_ * D + ch_ * 512: tt_ * D + ch_ * 512 + 512],
                    psum[pb_][:, :], 1.0,
                    xres[:, tt_ * D + ch_ * 512: tt_ * D + ch_ * 512 + 512],
                    AX.mult, AX.add))(pbi, tt, ch),
                    reads=(f"psum{pbi}", f"xres{tt}"), writes=(f"xres{tt}",))

        if debug and dbg_state["n"] == 0:
            P.op("sync", lambda e: e.dma_start(ddbg[7:8, :], xres[0:1, 0:D]),
                 reads=("xres0",), writes=("dbg_done9",), sem="dmg")

        if FENCES:
            P.fence()
        if upto >= 2:
            ffn_sublayer()

        if FENCES:
            P.fence()
        if upto < 3:
            for tt in range(NTT):
                P.op("sync", (lambda tt_: lambda e: e.dma_start(
                    dout[s, t0 + tt_ * 128: t0 + (tt_ + 1) * 128, :],
                    xres[:, tt_ * D:(tt_ + 1) * D]))(tt),
                    reads=(f"xres{tt}",), writes=(f"store_{s}_{c}_{tt}",), sem="dmx")
            P.close_phase("dmx")
            return
        layernorm()
        if ZERO_HS or ZERO_CARRY:
            P.op("vector", lambda e: e.memset(hcar_h[:, :], 0.0),
                 writes=tuple(f"hcar_h_{e8}" for e8 in range(8)))
        if ZERO_HS or ZERO_TAIL:
            P.op("vector", lambda e: e.memset(xatail[:, :], 0.0),
                 writes=tuple(f"xatail_{e8}" for e8 in range(8)))
        P.op("vector", lambda e: e.memset(
            alpha_f[:, :].rearrange("p (e c) -> p e c", c=SP)[:, :, 0:2], 0.0),
            writes=tuple(f"alpha_{gg}" for gg in range(8)))
        for pb in range(2):
            P.op("vector", (lambda pb_: lambda e: e.memset(
                uh_f[:, pb_ * SP:pb_ * SP + 1], 0.0))(pb),
                writes=(f"uh{pb}",))
        for k in range(8):
            P.op("sync", (lambda k_: lambda e: e.dma_start(
                wa[:, k_ * 2048:(k_ + 1) * 2048],
                hHinT[k_ * 128:(k_ + 1) * 128, :]))(k),
                reads=(f"hbm_hHinT{k}",), writes=(f"wa{k}",), sem="dmw")
        P.close_phase("dmw")
        for ee in range(16):
            pbi = ee % 2
            for k in range(8):
                mm(pbi, wa[:, k * 2048 + ee * 128: k * 2048 + (ee + 1) * 128],
                   xhat_f[:, k * TC:(k + 1) * TC], k == 0, k == 7,
                   extra_reads=(f"wa{k}",) + XHATF)
            if ee < 8:
                P.op("scalar", (lambda pb_, ee_: lambda e: e.copy(
                    gate_bf[:, ee_ * TC:(ee_ + 1) * TC], psum[pb_][:, :]))(pbi, ee),
                    reads=(f"psum{pbi}",), writes=(f"gate_{ee}",))
            else:
                e8 = ee - 8
                P.op("scalar", (lambda pb_, e8_: lambda e: e.copy(
                    xa_pad[:, e8_ * PAD + 3:(e8_ + 1) * PAD], psum[pb_][:, :]))(pbi, e8),
                    reads=(f"psum{pbi}",), writes=(f"xa_{e8}",))
        for e8 in range(8):
            pbi = e8 % 2
            P.op("vector", (lambda e8_: lambda e: e.tensor_copy(
                xa_pad[:, e8_ * PAD: e8_ * PAD + 3],
                xatail[:, e8_ * 3: e8_ * 3 + 3]))(e8),
                reads=(f"xatail_{e8}",), writes=(f"xa_pre_{e8}",))
            for k in range(4):
                mm(pbi, convd[:, (k * 8 + e8) * 128:(k * 8 + e8 + 1) * 128],
                   xa_pad[:, e8 * PAD + k: e8 * PAD + k + TC], k == 0, k == 3,
                   extra_reads=("convd", f"xa_{e8}", f"xa_pre_{e8}"))
            P.op("vector", (lambda pb_, e8_: lambda e: e.tensor_scalar(
                xc_bf[:, e8_ * TC:(e8_ + 1) * TC], psum[pb_][:, :],
                bconv[:, e8_:e8_ + 1], None, AX.add))(pbi, e8),
                reads=(f"psum{pbi}", "csml_f"), writes=(f"xc_{e8}",))
            P.op("vector", (lambda e8_: lambda e: e.tensor_copy(
                xatail[:, e8_ * 3: e8_ * 3 + 3],
                xa_pad[:, e8_ * PAD + TC: e8_ * PAD + TC + 3]))(e8),
                reads=(f"xa_{e8}",), writes=(f"xatail_{e8}",))
        for k in range(8):
            P.op("sync", (lambda k_: lambda e: e.dma_start(
                wa[:, k_ * 2048:(k_ + 1) * 2048],
                hHgT[k_ * 128:(k_ + 1) * 128, :]))(k),
                reads=(f"hbm_hHgT{k}",), writes=(f"wa{k}",), sem="dmw")
        P.close_phase("dmw")
        for gg in range(16):
            pbi = gg % 2
            for k in range(8):
                mm(pbi, wa[:, k * 2048 + gg * 128: k * 2048 + (gg + 1) * 128],
                   xc_bf[:, k * TC:(k + 1) * TC], k == 0, k == 7,
                   extra_reads=(f"wa{k}", f"xc_{k}"))
            if gg < 8:
                P.op("scalar", (lambda pb_, gg_: lambda e: e.activation(
                    sigf_f[:, :], psum[pb_][:, :], AF.Sigmoid,
                    bias=bgf[:, gg_:gg_ + 1]))(pbi, gg),
                    reads=(f"psum{pbi}", "csml_f"), writes=("sigf",))
                P.op("scalar", (lambda gg_: lambda e: e.activation(
                    alpha_f[:, gg_ * SP + 2:(gg_ + 1) * SP], sigf_f[:, :], AF.Exp,
                    scale=mfb[:, gg_:gg_ + 1]))(gg),
                    reads=("sigf", "mfb"), writes=(f"alpha_{gg}",))
            else:
                g8 = gg - 8
                P.op("scalar", (lambda pb_, g8_: lambda e: e.activation(
                    sigi_bf[:, g8_ * TC:(g8_ + 1) * TC], psum[pb_][:, :], AF.Sigmoid,
                    bias=bgi[:, g8_:g8_ + 1]))(pbi, g8),
                    reads=(f"psum{pbi}", "csml_f"), writes=(f"sigi_{g8}",))
        for e8 in range(8):
            pbuf = e8 % 2
            car = hcar_h[:, s * 8 + e8: s * 8 + e8 + 1]
            P.op("vector", (lambda pb_, car_: lambda e: e.tensor_copy(
                uh_f[:, pb_ * SP + 1:pb_ * SP + 2], car_))(pbuf, car),
                reads=(f"hcar_h_{e8}",), writes=(f"uh{pbuf}",))
            P.op("scalar", (lambda e8_: lambda e: e.activation(
                beta_f[:, :], alpha_f[:, e8_ * SP + 2:(e8_ + 1) * SP], AF.Square))(e8),
                reads=(f"alpha_{e8}",), writes=("beta",))
            P.op("scalar", lambda e: e.activation(beta_f[:, :], beta_f[:, :], AF.Ln,
                                                  scale=-1.0, bias=cst[:, 1:2]),
                 reads=("beta", "csml_f"), writes=("beta",))
            P.op("scalar", lambda e: e.activation(beta_f[:, :], beta_f[:, :],
                                                  AF.Exp, scale=0.5),
                 reads=("beta",), writes=("beta",))
            P.op("vector", (lambda e8_, pb_: lambda e: e.tensor_mul(
                uh_f[:, pb_ * SP + 2:(pb_ + 1) * SP], beta_f[:, :],
                sigi_bf[:, e8_ * TC:(e8_ + 1) * TC]))(e8, pbuf),
                reads=("beta", f"sigi_{e8}"), writes=(f"uh{pbuf}",))
            P.op("vector", (lambda e8_, pb_: lambda e: e.tensor_mul(
                uh_f[:, pb_ * SP + 2:(pb_ + 1) * SP],
                uh_f[:, pb_ * SP + 2:(pb_ + 1) * SP],
                xc_bf[:, e8_ * TC:(e8_ + 1) * TC]))(e8, pbuf),
                reads=(f"uh{pbuf}", f"xc_{e8}"), writes=(f"uh{pbuf}",))
            P.op("vector", (lambda e8_, pb_: lambda e: e.tensor_tensor_scan(
                hh_bf[:, e8_ * SP:(e8_ + 1) * SP],
                alpha_f[:, e8_ * SP:(e8_ + 1) * SP],
                uh_f[:, pb_ * SP:(pb_ + 1) * SP],
                junk[:, 0:1], AX.mult, AX.add))(e8, pbuf),
                reads=(f"alpha_{e8}", f"uh{pbuf}"),
                writes=(f"hh_{e8}",))
            P.op("scalar", (lambda e8_: lambda e: e.activation(
                ge_f[:, :], gate_bf[:, e8_ * TC:(e8_ + 1) * TC], AF.Gelu))(e8),
                reads=(f"gate_{e8}",), writes=("ge_f",))
            P.op("vector", (lambda e8_: lambda e: e.tensor_mul(
                yh_bf[:, e8_ * TC:(e8_ + 1) * TC], ge_f[:, :],
                hh_bf[:, e8_ * SP + 2:(e8_ + 1) * SP]))(e8),
                reads=("ge_f", f"hh_{e8}"), writes=(f"yh_{e8}",))
            P.op("vector", (lambda e8_, car_: lambda e: e.tensor_copy(
                car_, hh_bf[:, (e8_ + 1) * SP - 1:(e8_ + 1) * SP]))(e8, car),
                reads=(f"hh_{e8}",), writes=(f"hcar_h_{e8}",))
        for k in range(8):
            P.op("sync", (lambda k_: lambda e: e.dma_start(
                wa[:, k_ * 1024:(k_ + 1) * 1024],
                hHoT[k_ * 128:(k_ + 1) * 128, :]))(k),
                reads=(f"hbm_hHoT{k}",), writes=(f"wa{k}",), sem="dmw")
        P.close_phase("dmw")
        for tt in range(NTT):
            for ch in range(2):
                pbi = 2 + ch
                for k in range(8):
                    mm(pbi, yh_bf[:, k * TC + tt * 128: k * TC + (tt + 1) * 128],
                       wa[:, k * 1024 + ch * 512: k * 1024 + ch * 512 + 512],
                       k == 0, k == 7,
                       extra_reads=(f"yh_{k}", f"wa{k}"))
                P.op("vector", (lambda pb_, tt_, ch_: lambda e: e.scalar_tensor_tensor(
                    xres[:, tt_ * D + ch_ * 512: tt_ * D + ch_ * 512 + 512],
                    psum[pb_][:, :], 1.0,
                    xres[:, tt_ * D + ch_ * 512: tt_ * D + ch_ * 512 + 512],
                    AX.mult, AX.add))(pbi, tt, ch),
                    reads=(f"psum{pbi}", f"xres{tt}"), writes=(f"xres{tt}",))

        if FENCES:
            P.fence()
        if upto >= 4:
            ffn_sublayer()

        for tt in range(NTT):
            P.op("sync", (lambda tt_: lambda e: e.dma_start(
                dout[s, t0 + tt_ * 128: t0 + (tt_ + 1) * 128, :],
                xres[:, tt_ * D:(tt_ + 1) * D]))(tt),
                reads=(f"xres{tt}",), writes=(f"store_{s}_{c}_{tt}",), sem="dmx")
        P.close_phase("dmx")

    if FENCES:
        P.fence()
    for s in range(NSEQ):
        P.op("vector", (lambda s_: lambda e: e.memset(
            hcar_m[:, s_ * 256:(s_ + 1) * 256], 0.0))(s),
            writes=tuple(f"car_{dt}_{n}" for dt in range(16) for n in range(16)))
        P.op("vector", (lambda s_: lambda e: e.memset(
            hcar_h[:, s_ * 8:(s_ + 1) * 8], 0.0))(s),
            writes=tuple(f"hcar_h_{e8}" for e8 in range(8)))
        for e8 in range(8):
            P.op("vector", (lambda e8_: lambda e: e.memset(
                xatail[:, e8_ * 3: e8_ * 3 + 3], 0.0))(e8),
                writes=(f"xatail_{e8}",))
        for c in range(NCH):
            if FENCES:
                P.fence()
            chunk(s, c)

    fin = P.op("sync", lambda e: None,
               reads=tuple(f"store_{s}_{c}_{tt}" for s in range(NSEQ)
                           for c in range(NCH) for tt in range(NTT)))
    fin["dma"] = False

    P.finalize()

    with nc.Block() as blk:
        @blk.sync
        def _(e):
            P.emit("sync", e, sems)

        @blk.gpsimd
        def _(e):
            P.emit("gpsimd", e, sems)

        @blk.vector
        def _(e):
            P.emit("vector", e, sems)

        @blk.scalar
        def _(e):
            P.emit("scalar", e, sems)

        @blk.tensor
        def _(e):
            P.emit("tensor", e, sems)

    es.close()
    return nc


_NC_CACHE = [None]


def kernel(**inputs):
    x = np.ascontiguousarray(inputs["L0_attn_a_x"])
    base = host_inputs(inputs)
    if _NC_CACHE[0] is None:
        _NC_CACHE[0] = build_nc()
    nc = _NC_CACHE[0]
    in_maps = []
    for core in range(8):
        m = dict(base)
        m["x"] = np.ascontiguousarray(x[2 * core:2 * core + 2])
        in_maps.append(m)
    res = run_bass_kernel_spmd(nc, in_maps, core_ids=list(range(8)))
    out = np.concatenate([r["out"] for r in res.results], axis=0)
    return np.ascontiguousarray(out.astype(np.float32))


def host_inputs(inputs):
    w_conv = np.asarray(inputs["L0_attn_b_w_conv"])
    convd = np.zeros((128, 4096), dtype=np.float32)
    idx = np.arange(128)
    for k in range(4):
        for e8 in range(8):
            convd[:, (k * 8 + e8) * 128:(k * 8 + e8 + 1) * 128][idx, idx] = \
                w_conv[e8 * 128:(e8 + 1) * 128, 0, k]
    smf = np.zeros((128, 512), dtype=np.float32)
    smf[:, 0] = 1e-5
    smf[:, 1] = 1.0 + 1e-6
    smf[:, 2] = 1.0
    smf[:, 3] = np.asarray(inputs["L0_ffn_input_scale"])[0]
    smf[:, 4] = np.asarray(inputs["L0_ffn_kernel_scale"])[0]
    smf[:, 8:24] = np.asarray(inputs["L0_attn_a_b_delta"]).reshape(16, 128).T
    smf[:, 24:40] = np.asarray(inputs["L0_attn_a_D_param"]).reshape(16, 128).T
    smf[:, 40:48] = np.asarray(inputs["L0_attn_b_b_conv"]).reshape(8, 128).T
    bg = np.asarray(inputs["L0_attn_b_b_gates"])
    smf[:, 48:56] = bg[:1024].reshape(8, 128).T
    smf[:, 56:64] = bg[1024:].reshape(8, 128).T
    smf[:, 64:72] = np.asarray(inputs["L0_attn_b_forget_base"]).reshape(8, 128).T
    alog = np.asarray(inputs["L0_attn_a_A_log"])
    smf[:, 256:512] = alog.reshape(16, 128, 16).transpose(1, 0, 2).reshape(128, 256)
    smb = np.zeros((128, 4096), dtype=np.float32)
    smb[0, 0:128] = 1.0
    smb_bf = smb.astype(ml_dtypes.bfloat16)
    smb_bf[0, 128:128 + 1024] = np.asarray(inputs["L0_ffn_bias"])
    smb_bf[:, 1152:1280] = np.eye(128, dtype=np.float32).astype(ml_dtypes.bfloat16)
    sel = np.zeros((16, 2048), dtype=np.float32)
    for n in range(16):
        sel[n, n * 128:(n + 1) * 128] = 1.0
    smb_bf[0:16, 2048:4096] = sel.astype(ml_dtypes.bfloat16)
    return dict(
        a_W_in=np.ascontiguousarray(inputs["L0_attn_a_W_in"]),
        a_W_delta=np.ascontiguousarray(inputs["L0_attn_a_W_delta"]),
        a_W_B=np.ascontiguousarray(inputs["L0_attn_a_W_B"]),
        a_W_C=np.ascontiguousarray(inputs["L0_attn_a_W_C"]),
        a_W_out=np.ascontiguousarray(inputs["L0_attn_a_W_out"]),
        b_w_in=np.ascontiguousarray(inputs["L0_attn_b_w_in"]),
        b_w_gates=np.ascontiguousarray(inputs["L0_attn_b_w_gates"]),
        b_w_out=np.ascontiguousarray(inputs["L0_attn_b_w_out"]),
        f_kernel=np.ascontiguousarray(inputs["L0_ffn_kernel"]),
        c_convd=convd,
        c_small_f=smf,
        c_small_b=np.ascontiguousarray(smb_bf),
        c_biasrep=np.ascontiguousarray(
            np.broadcast_to(np.asarray(inputs["L0_ffn_bias"]), (128, 1024)).copy()),
    )
